# version 02 (final)
# baseline (speedup 1.0000x reference)
import numpy as np
import ml_dtypes
import concourse.bass as bass
import concourse.mybir as mybir
from concourse.bass_utils import run_bass_kernel_spmd

F32 = mybir.dt.float32
BF16 = mybir.dt.bfloat16
AF = mybir.ActivationFunctionType
OP = mybir.AluOpType
AX = mybir.AxisListType.X

D = 1024
NT = 2048
HALO = 256
TH = NT + 2 * HALO
NC_ = 8
NEG = -1.0e9
ENGS = ("tensor", "scalar", "vector", "sync", "gpsimd")

BV = {}
_c = 0
for _n, _w in (("bq", 8), ("bk", 8), ("bv", 8), ("bo", 8), ("lnw", 64),
               ("lnb", 64), ("fc1b", 256), ("fc2b", 64), ("alpha", 1),
               ("qkvbq", 8), ("qkvbk", 8), ("qkvbv", 8), ("outb", 8),
               ("b1", 32), ("b2", 8), ("g1", 8), ("be1", 8), ("g2", 8),
               ("be2", 8), ("b1e", 128), ("b2e", 64)):
    BV[_n] = _c
    _c += _w
BVW = 832


class P:
    def __init__(self, nc):
        self.nc = nc
        self.streams = {e: [] for e in ENGS}
        self.counts = {e: 0 for e in ENGS}
        self.pending = {e: [] for e in ENGS}
        self.sems = None

    def em(self, eng, fn, deps=(), inc=1):
        deps = [d for d in deps if d is not None] + self.pending[eng]
        self.pending[eng] = []
        self.streams[eng].append((deps, fn, inc))
        self.counts[eng] += inc
        return (eng, self.counts[eng])

    def dma(self, eng, out, in_, deps=()):
        return self.em(eng, lambda e, o=out, i=in_: e.dma_start(out=o, in_=i),
                       deps=deps, inc=16)

    def barrier(self):
        toks = [(e, self.counts[e]) for e in ENGS if self.counts[e] > 0]
        for e in ENGS:
            self.pending[e] += [t for t in toks if t[0] != e]

    def finalize(self, block):
        sems = self.sems

        def mk(ename):
            def run(eng_h):
                waited = {}
                sem = sems[ename]
                for deps, fn, inc in self.streams[ename]:
                    for (de, dv) in deps:
                        if waited.get(de, -1) < dv:
                            eng_h.wait_ge(sems[de], dv)
                            waited[de] = dv
                    ins = fn(eng_h)
                    if ins is not None:
                        ins.then_inc(sem, inc)
                eng_h.wait_ge(sem, self.counts[ename])
            return run
        block.tensor(mk("tensor"))
        block.scalar(mk("scalar"))
        block.vector(mk("vector"))
        block.sync(mk("sync"))
        block.gpsimd(mk("gpsimd"))


def ts(i, n=128):
    return slice(i * n, (i + 1) * n)


def build(part):
    nc = bass.Bass("TRN2", target_bir_lowering=False)
    p = P(nc)
    inp = {}
    outs = {}

    def din(name, shape, dt=F32):
        inp[name] = nc.dram_tensor(name, list(shape), dt,
                                   kind="ExternalInput").ap()

    def dout(name, shape, dt=F32):
        outs[name] = nc.dram_tensor(name, list(shape), dt,
                                    kind="ExternalOutput").ap()

    if part == "A":
        din("xt", [D, TH])
        for w in ("wq", "wk", "wv", "wo"):
            din(w, [D, D])
        din("Wg", [D, 8])
        din("fc1w", [8, D, 4096])
        din("fc2w", [8, 4096, D])
        din("qkvw", [D, 3072])
        din("mask", [128, 5 * 768], BF16)
        dout("xmid", [D, NT], BF16)
        dout("kvz", [128, 1040])
    else:
        din("xmid", [D, NT], BF16)
        din("kvz0", [128, 1040])
        din("kvz1", [128, 1040])
        din("qkvw", [D, 3072])
        din("outw", [D, D])
        din("w1", [D, 4096])
        din("w2", [4096, D])
        din("WgB", [D, 8])
        din("W1e", [8, D, 2048])
        din("W2e", [8, 2048, D])
        dout("out", [D, NT])
    din("bvec", [128, BVW])
    din("idbf", [128, 128], BF16)
    din("id32", [128, 128], F32)
    din("onesbf", [128, 128], BF16)

    ctx = []

    def sb(name, shape, dt):
        h = nc.sbuf_tensor(name, list(shape), dt)
        t = h.__enter__()
        ctx.append(h)
        return t

    def psum(name, shape):
        h = nc.psum_tensor(name, list(shape), F32)
        t = h.__enter__()
        ctx.append(h)
        return t

    XT = sb("XT", [128, NC_ * NT], BF16)
    LNT = sb("LNT", [128, NC_ * (TH if part == "A" else NT)], BF16)
    T2 = sb("T2", [128, 16384], BF16)
    T3 = sb("T3", [128, 16384], BF16)
    T4 = sb("T4", [128, 8192], BF16)
    LNE = sb("LNE", [128, 4096], BF16)
    BCr = sb("BCr", [128, 512], BF16)
    BCm = sb("BCm", [128, 512], BF16)
    STGa = sb("STGa", [128, 1024], F32)
    STGb = sb("STGb", [128, 1024], F32)
    MSK = sb("MSK", [128, 5 * 768], BF16) if part == "A" else None
    BVEC = sb("BVEC", [128, BVW], F32)
    IDB = sb("IDB", [128, 128], BF16)
    ID3 = sb("ID3", [128, 128], F32)
    ONE = sb("ONE", [128, 128], BF16)
    CMT = sb("CMT", [8, 2048], BF16)
    CBC = sb("CBC", [128, 512], BF16)
    LOG = sb("LOG", [128, 16 * 8], F32)
    GSC = sb("GSC", [128, 64], F32)
    KVB = sb("KVB", [128, 1040], BF16) if part == "B" else None
    KV2a = sb("KV2a", [128, 1040], F32) if part == "B" else None
    KV2b = sb("KV2b", [128, 1040], F32) if part == "B" else None
    KVS = sb("KVS", [128, 130], F32) if part == "A" else None
    CROW = sb("CROW", [1, 2048], BF16)
    ROWa = sb("ROWa", [1, 512], BF16)
    ROWb = sb("ROWb", [1, 512], BF16)
    SR = sb("SR", [1, 1536], F32)
    PTB = sb("PTB", [128, 768], BF16)
    PSB2 = sb("PSB2", [128, 768], BF16)

    psA0 = psum("psA0", [128, 512])
    psA1 = psum("psA1", [128, 512])
    psY = psum("psY", [128, 2048])
    psM0 = psum("psM0", [128, 512])
    h = nc.psum_tensor("psT", [128, 1024], BF16)
    psT = h.__enter__()
    ctx.append(h)
    last = {"psA0": None, "psA1": None, "psY": None, "psM0": None,
            "psT": None}

    def bvc(name, i=0):
        return BVEC[:, BV[name] + i: BV[name] + i + 1]

    stg = {"i": 0, "free": [None, None]}

    def loadcast(src_ap, dst_ap, n, deps=()):
        ctok = None
        for lo in range(0, n, 1024):
            m = min(1024, n - lo)
            i = stg["i"]
            stg["i"] ^= 1
            s = STGa if i == 0 else STGb
            dtok = p.dma("sync", s[:, 0:m], src_ap[:, lo:lo + m],
                         deps=list(deps) + [stg["free"][i]])
            ctok = p.em("scalar", lambda e, s=s, d=dst_ap, lo=lo, m=m:
                        e.activation(out=d[:, lo:lo + m], in_=s[:, 0:m],
                                     func=AF.Copy),
                        deps=[dtok])
            stg["free"][i] = ctok
        return ctok

    def mmgroup(out_ap, pairs, slot, deps=()):
        first = list(deps)
        if last[slot] is not None:
            first.append(last[slot])
        n = len(pairs)
        tok = None
        for k, (lh, rh) in enumerate(pairs):
            tok = p.em("tensor",
                       lambda e, o=out_ap, l=lh, r=rh, s=(k == 0),
                       st=(k == n - 1): e.matmul(o, l, r, start=s, stop=st),
                       deps=first if k == 0 else ())
        return tok

    def evac(slot, tok):
        last[slot] = tok

    def transp(in_ap, out_ap, deps=()):
        tok = p.em("tensor", lambda e, i=in_ap: e.transpose(
            psT[:, 0:128], i, IDB[:]), deps=list(deps) + [last["psT"]])
        ctok = p.em("scalar", lambda e, o=out_ap: e.copy(o, psT[:, 0:128]),
                    deps=[tok])
        evac("psT", ctok)
        return ctok

    def fmln2(src, dst, T, g=None, b=None):
        ftoks = []
        nch = (T + 511) // 512
        for t in range(nch):
            lo = t * 512
            n = min(512, T - lo)
            mm_s = None
            for c in range(NC_):
                mm_s = p.em("tensor",
                            lambda e, c=c, lo=lo, n=n, s=(c == 0),
                            st=(c == NC_ - 1): e.matmul(
                                psM0[0:1, 0:n], ONE[:, 0:1], src(c, lo, n),
                                start=s, stop=st),
                            deps=[last["psM0"]] + ftoks[-1:] if c == 0
                            else ())
            t1 = p.em("vector", lambda e, n=n: e.tensor_scalar(
                SR[0:1, 0:n], psM0[0:1, 0:n], 1.0 / D, None, OP.mult),
                deps=[mm_s])
            evac("psM0", t1)
            sq = None
            mm_q = None
            for c in range(NC_):
                sq = p.em("scalar", lambda e, c=c, lo=lo, n=n: e.activation(
                    out=PSB2[:, 0:n], in_=src(c, lo, n), func=AF.Square),
                    deps=[t1] + ftoks[-1:] if c == 0 else [mm_q])
                mm_q = p.em("tensor", lambda e, n=n, s=(c == 0),
                            st=(c == NC_ - 1): e.matmul(
                                psM0[0:1, 0:n], ONE[:, 0:1], PSB2[:, 0:n],
                                start=s, stop=st),
                            deps=[sq, last["psM0"]] if c == 0 else [sq])
            t2 = p.em("vector", lambda e, n=n: e.tensor_scalar(
                SR[0:1, 512:512 + n], psM0[0:1, 0:n], 1.0 / D, None,
                OP.mult), deps=[mm_q])
            evac("psM0", t2)
            t3 = p.em("vector", lambda e, n=n: e.scalar_tensor_tensor(
                PSB2[0:1, 0:n], SR[0:1, 0:n], -1.0, SR[0:1, 0:n],
                OP.mult, OP.mult), deps=[t2])
            t4 = p.em("vector", lambda e, n=n: e.scalar_tensor_tensor(
                SR[0:1, 512:512 + n], SR[0:1, 512:512 + n], 1.0e-5,
                PSB2[0:1, 0:n], OP.add, OP.add), deps=[t3])
            t5a = p.em("scalar", lambda e, n=n: e.activation(
                out=SR[0:1, 512:512 + n], in_=SR[0:1, 512:512 + n],
                func=AF.Sqrt), deps=[t4])
            t5r = p.em("vector", lambda e, n=n: e.reciprocal(
                SR[0:1, 1024:1024 + n], SR[0:1, 512:512 + n]), deps=[t5a])
            t5 = p.em("scalar", lambda e, n=n: e.copy(
                ROWa[0:1, 0:n], SR[0:1, 1024:1024 + n]), deps=[t5r])
            t6 = p.em("vector", lambda e, n=n: e.scalar_tensor_tensor(
                ROWb[0:1, 0:n], SR[0:1, 0:n], -1.0, ROWa[0:1, 0:n],
                OP.mult, OP.mult), deps=[t5])
            b1t = mmgroup(psM0[:, 0:n], [(ONE[0:1, :], ROWa[0:1, 0:n])],
                          "psM0", deps=[t5])
            c1 = p.em("scalar", lambda e, n=n: e.copy(BCr[:, 0:n],
                                                      psM0[:, 0:n]),
                      deps=[b1t])
            evac("psM0", c1)
            b2t = mmgroup(psM0[:, 0:n], [(ONE[0:1, :], ROWb[0:1, 0:n])],
                          "psM0", deps=[t6])
            c2 = p.em("scalar", lambda e, n=n: e.copy(BCm[:, 0:n],
                                                      psM0[:, 0:n]),
                      deps=[b2t])
            evac("psM0", c2)
            for c in range(NC_):
                if g is None:
                    x1 = p.em("vector", lambda e, c=c, lo=lo, n=n:
                              e.scalar_tensor_tensor(
                                  dst(c, lo, n), src(c, lo, n), 1.0,
                                  BCr[:, 0:n], OP.mult, OP.mult), deps=[c1])
                    x2 = p.em("vector", lambda e, c=c, lo=lo, n=n:
                              e.tensor_add(dst(c, lo, n), dst(c, lo, n),
                                           BCm[:, 0:n]), deps=[c2, x1])
                    ftoks.append(x2)
                else:
                    x1 = p.em("vector", lambda e, c=c, lo=lo, n=n:
                              e.scalar_tensor_tensor(
                                  PTB[:, 0:n], src(c, lo, n), 1.0,
                                  BCr[:, 0:n], OP.mult, OP.mult),
                              deps=[c1] + ftoks[-1:])
                    x2 = p.em("vector", lambda e, n=n:
                              e.tensor_add(PTB[:, 0:n], PTB[:, 0:n],
                                           BCm[:, 0:n]), deps=[c2, x1])
                    x3 = p.em("scalar", lambda e, c=c, lo=lo, n=n:
                              e.activation(out=dst(c, lo, n), in_=PTB[:, 0:n],
                                           func=AF.Identity, scale=g(c),
                                           bias=b(c)), deps=[x2])
                    ftoks.append(x3)
        return ftoks

    bv_tok = p.dma("gpsimd", BVEC[:], inp["bvec"])
    base = [bv_tok, p.dma("gpsimd", IDB[:], inp["idbf"]),
            p.dma("gpsimd", ID3[:], inp["id32"]),
            p.dma("gpsimd", ONE[:], inp["onesbf"])]
    if part == "A":
        base.append(p.dma("gpsimd", MSK[:], inp["mask"]))

    if part == "A":
        xl = []
        for c in range(NC_):
            for piece in range(4):
                lo = piece * (TH // 4)
                n = TH // 4
                i = stg["i"]
                stg["i"] ^= 1
                s = STGa if i == 0 else STGb
                dt = p.dma("gpsimd", s[:, 0:n], inp["xt"][ts(c), lo:lo + n],
                           deps=[stg["free"][i]] + base)
                ct = p.em("scalar", lambda e, c=c, lo=lo, n=n, s=s:
                          e.activation(out=LNT[:, c * TH + lo: c * TH + lo + n],
                                       in_=s[:, 0:n], func=AF.Copy),
                          deps=[dt])
                stg["free"][i] = ct
                xl.append(ct)
        for c in range(NC_):
            xl.append(p.em("vector", lambda e, c=c: e.tensor_copy(
                XT[:, ts(c, NT)],
                LNT[:, c * TH + HALO: c * TH + HALO + NT]), deps=xl))
        p.barrier()
        fmln2(lambda c, lo, n: LNT[:, c * TH + lo: c * TH + lo + n],
              lambda c, lo, n: LNT[:, c * TH + lo: c * TH + lo + n], TH)
        p.barrier()

        def lnh(c, lo, n):
            return LNT[:, c * TH + lo: c * TH + lo + n]

        OSB = T2
        WS = 12288
        for hg in range(NC_):
            wtoks = []
            for wi, wname in enumerate(("wq", "wk", "wv")):
                for c in range(NC_):
                    wtoks.append(loadcast(
                        inp[wname][ts(c), ts(hg)],
                        T3[:, WS + (wi * NC_ + c) * 128:
                           WS + (wi * NC_ + c) * 128 + 128], 128))

            def wsl(wi, c):
                return T3[:, WS + (wi * NC_ + c) * 128:
                          WS + (wi * NC_ + c) * 128 + 128]
            qk_toks = []
            for tcol in range(4):
                tok = mmgroup(psA0[:, :],
                              [(wsl(0, c), lnh(c, HALO + tcol * 512, 512))
                               for c in range(NC_)], "psA0", deps=wtoks)
                et = p.em("scalar", lambda e, tcol=tcol, hg=hg: e.activation(
                    out=T3[:, tcol * 512: tcol * 512 + 512], in_=psA0[:, :],
                    func=AF.Identity, bias=bvc("bq", hg)), deps=[tok])
                evac("psA0", et)
                qk_toks.append(et)
            for tcol in range(5):
                tok = mmgroup(psA1[:, :],
                              [(wsl(1, c), lnh(c, tcol * 512, 512))
                               for c in range(NC_)], "psA1", deps=wtoks)
                et = p.em("scalar", lambda e, tcol=tcol, hg=hg: e.activation(
                    out=T3[:, 2048 + tcol * 512: 2048 + tcol * 512 + 512],
                    in_=psA1[:, :], func=AF.Identity, bias=bvc("bk", hg)),
                    deps=[tok])
                evac("psA1", et)
                qk_toks.append(et)
            vt_toks = []
            for tcol in range(5):
                tok = mmgroup(psA0[:, :],
                              [(wsl(2, c), lnh(c, tcol * 512, 512))
                               for c in range(NC_)], "psA0", deps=wtoks)
                et = p.em("scalar", lambda e, tcol=tcol, hg=hg: e.activation(
                    out=T3[:, 4608 + tcol * 512: 4608 + tcol * 512 + 512],
                    in_=psA0[:, :], func=AF.Identity, bias=bvc("bv", hg)),
                    deps=[tok])
                evac("psA0", et)
                vt_toks.append(et)
            vtr = []
            for t in range(20):
                vtr.append(transp(T3[:, 4608 + t * 128: 4608 + t * 128 + 128],
                                  T4[:, t * 128: t * 128 + 128],
                                  deps=vt_toks))
            prev_ex = []
            prev_oe = None
            for h in range(2):
                for qt in range(16):
                    wlo = qt * 128
                    wn = min(768, TH - wlo)
                    qsl = T3[h * 64:h * 64 + 64, qt * 128: qt * 128 + 128]
                    pieces = []
                    stoks = []
                    off = 0
                    while off < wn:
                        m = min(512, wn - off)
                        stoks.append(mmgroup(
                            psY[:, off:off + m],
                            [(qsl, T3[h * 64:h * 64 + 64,
                                      2048 + wlo + off:2048 + wlo + off + m])],
                            "psY", deps=qk_toks + vtr))
                        pieces.append((off, m))
                        off += m
                    mv = {0: 1, 1: 2, 14: 3, 15: 4}.get(qt, 0)
                    mtoks = []
                    for (off, m) in pieces:
                        mtoks.append(p.em("vector",
                                     lambda e, off=off, m=m, mv=mv:
                                     e.scalar_tensor_tensor(
                                         psY[:, off:off + m],
                                         psY[:, off:off + m], 0.125,
                                         MSK[:, mv * 768 + off:
                                             mv * 768 + off + m],
                                         OP.mult, OP.add), deps=stoks))
                    rm = []
                    for i, (off, m) in enumerate(pieces):
                        rm.append(p.em("vector",
                                  lambda e, off=off, m=m, i=i: e.reduce_max(
                                      GSC[:, i:i + 1], psY[:, off:off + m],
                                      AX, negate=True),
                                  deps=mtoks + prev_ex))
                    nm = rm[-1]
                    if len(pieces) > 1:
                        nm = p.em("vector", lambda e: e.tensor_tensor(
                            GSC[:, 0:1], GSC[:, 0:1], GSC[:, 1:2], OP.min),
                            deps=rm)
                    ex = []
                    for (off, m) in pieces:
                        ex.append(p.em("scalar",
                                  lambda e, off=off, m=m: e.activation(
                                      out=PSB2[:, off:off + m],
                                      in_=psY[:, off:off + m], func=AF.Exp,
                                      bias=GSC[:, 0:1]), deps=[nm]))
                    evac("psY", ex[-1])
                    prev_ex = ex
                    rs_ = p.em("vector", lambda e, wn=wn: e.reduce_sum(
                        GSC[:, 2:3], PSB2[:, 0:wn], AX),
                        deps=ex + ([prev_oe] if prev_oe else []))
                    rr = p.em("vector", lambda e: e.reciprocal(
                        GSC[:, 3:4], GSC[:, 2:3]), deps=[rs_])
                    nko = wn // 128
                    ptt = []
                    for ko in range(nko):
                        ptt.append(transp(PSB2[:, ko * 128:(ko + 1) * 128],
                                          PTB[:, ko * 128: ko * 128 + 128],
                                          deps=ex))
                    otok = mmgroup(
                        psM0[:, 0:64],
                        [(PTB[:, ko * 128: ko * 128 + 128],
                          T4[:, (qt + ko) * 128 + h * 64:
                             (qt + ko) * 128 + h * 64 + 64])
                         for ko in range(nko)], "psM0", deps=ptt)
                    oe = p.em("scalar",
                              lambda e, qt=qt, hg=hg, h=h: e.activation(
                                  out=OSB[:, qt * 1024 + hg * 128 + h * 64:
                                          qt * 1024 + hg * 128 + h * 64 + 64],
                                  in_=psM0[:, 0:64], func=AF.Copy,
                                  scale=GSC[:, 3:4]), deps=[otok, rr])
                    evac("psM0", oe)
                    prev_oe = oe
            p.barrier()
        ot_toks = []
        for tt in range(16):
            for c in range(NC_):
                ot_toks.append(transp(
                    OSB[:, tt * 1024 + c * 128: tt * 1024 + c * 128 + 128],
                    T3[:, c * 2048 + tt * 128: c * 2048 + tt * 128 + 128]))
        for ec in range(NC_):
            p.barrier()
            wtoks = [loadcast(inp["wo"][ts(c), ts(ec)],
                              T4[:, c * 128: c * 128 + 128], 128)
                     for c in range(NC_)]
            for tcol in range(4):
                tok = mmgroup(psA0[:, :],
                              [(T4[:, c * 128: c * 128 + 128],
                                T3[:, c * 2048 + tcol * 512:
                                   c * 2048 + tcol * 512 + 512])
                               for c in range(NC_)], "psA0",
                              deps=wtoks + ot_toks)
                et = p.em("vector", lambda e, ec=ec, tcol=tcol:
                          e.scalar_tensor_tensor(
                              XT[:, ec * NT + tcol * 512:
                                 ec * NT + tcol * 512 + 512],
                              psA0[:, :], bvc("bo", ec),
                              XT[:, ec * NT + tcol * 512:
                                 ec * NT + tcol * 512 + 512],
                              OP.add, OP.add), deps=[tok])
                evac("psA0", et)
        p.barrier()

        fmln2(lambda c, lo, n: XT[:, c * NT + lo: c * NT + lo + n],
              lambda c, lo, n: LNT[:, c * NT + lo: c * NT + lo + n], NT)
        p.barrier()

        def lnc(c, lo, n):
            return LNT[:, c * NT + lo: c * NT + lo + n]
        wg_toks = [loadcast(inp["Wg"][ts(c), :], T4[:, c * 8: c * 8 + 8], 8)
                   for c in range(NC_)]
        gtoks = []
        for tt in range(16):
            tok = mmgroup(psM0[:, 0:8],
                          [(lnc(c, tt * 128, 128), T4[:, c * 8: c * 8 + 8])
                           for c in range(NC_)], "psM0", deps=wg_toks)
            gtoks.append(p.em("scalar", lambda e, tt=tt: e.copy(
                LOG[:, tt * 8: tt * 8 + 8], psM0[:, 0:8]), deps=[tok]))
            evac("psM0", gtoks[-1])
        cmb_toks = []
        prev_tt2 = None
        for tt in range(16):
            lg = LOG[:, tt * 8: tt * 8 + 8]
            t1 = p.em("vector", lambda e, lg=lg: e.reduce_max(
                GSC[:, 8:9], lg, AX, negate=True),
                deps=gtoks + ([prev_tt2] if prev_tt2 else []))
            t2 = p.em("vector", lambda e, lg=lg: e.tensor_scalar(
                GSC[:, 16:24], lg, GSC[:, 8:9], 0.0, OP.add, OP.is_equal),
                deps=[t1])
            t3 = p.em("vector", lambda e, lg=lg: e.scalar_tensor_tensor(
                GSC[:, 24:32], GSC[:, 16:24], NEG, lg, OP.mult, OP.add),
                deps=[t2])
            t4 = p.em("vector", lambda e: e.reduce_max(
                GSC[:, 9:10], GSC[:, 24:32], AX, negate=True), deps=[t3])
            t5 = p.em("vector", lambda e: e.tensor_sub(
                GSC[:, 10:11], GSC[:, 8:9], GSC[:, 9:10]), deps=[t4])
            t6 = p.em("scalar", lambda e: e.activation(
                out=GSC[:, 11:12], in_=GSC[:, 10:11], func=AF.Exp),
                deps=[t5])
            t7 = p.em("vector", lambda e: e.tensor_scalar(
                GSC[:, 12:13], GSC[:, 11:12], 1.0, None, OP.add), deps=[t6])
            t8 = p.em("vector", lambda e: e.reciprocal(
                GSC[:, 13:14], GSC[:, 12:13]), deps=[t7])
            t9 = p.em("vector", lambda e: e.tensor_mul(
                GSC[:, 14:15], GSC[:, 11:12], GSC[:, 13:14]), deps=[t8])
            ta = p.em("vector", lambda e, lg=lg: e.tensor_scalar(
                GSC[:, 32:40], lg, GSC[:, 9:10], 0.0, OP.add, OP.is_equal),
                deps=[t4])
            tb = p.em("vector", lambda e: e.tensor_scalar(
                GSC[:, 40:48], GSC[:, 32:40], GSC[:, 14:15], None, OP.mult),
                deps=[t9, ta])
            tc = p.em("vector", lambda e: e.scalar_tensor_tensor(
                GSC[:, 48:56], GSC[:, 16:24], GSC[:, 13:14], GSC[:, 40:48],
                OP.mult, OP.add), deps=[tb])
            tt1 = p.em("tensor", lambda e: e.transpose(
                psM0[0:8, 0:128], GSC[:, 48:56], ID3[:]),
                deps=[tc, last["psM0"]])
            tt2 = p.em("scalar", lambda e, tt=tt: e.copy(
                CMT[0:8, tt * 128: tt * 128 + 128], psM0[0:8, 0:128]),
                deps=[tt1])
            evac("psM0", tt2)
            prev_tt2 = tt2
            cmb_toks.append(tt2)
        al = p.em("vector", lambda e: e.tensor_scalar(
            CMT[0:8, :], CMT[0:8, :],
            BVEC[0:8, BV["alpha"]:BV["alpha"] + 1], None, OP.mult),
            deps=cmb_toks)
        p.barrier()

        prev_fc1_last = None
        prev_ev = []
        prev_cb = None
        for e_i in range(8):
            crow_tok = p.dma("gpsimd", CROW[0:1, :], CMT[e_i:e_i + 1, :],
                             deps=[al, prev_cb])
            for sub in range(2):
                w1toks = [loadcast(
                    inp["fc1w"][e_i, ts(c), sub * 2048:(sub + 1) * 2048],
                    T2[:, c * 2048:(c + 1) * 2048], 2048)
                    for c in range(NC_)]
                w2toks = [loadcast(
                    inp["fc2w"][e_i, sub * 2048 + f * 128:
                                sub * 2048 + (f + 1) * 128, :],
                    T3[:, f * 1024:(f + 1) * 1024], 1024)
                    for f in range(16)]
                for tg in range(4):
                    le_toks = [p.em("scalar",
                               lambda e, c=c, tg=tg, e_i=e_i: e.activation(
                                   out=LNE[:, c * 512:(c + 1) * 512],
                                   in_=lnc(c, tg * 512, 512), func=AF.Identity,
                                   scale=bvc("lnw", e_i * 8 + c),
                                   bias=bvc("lnb", e_i * 8 + c)),
                               deps=[bv_tok, prev_fc1_last])
                               for c in range(NC_)]
                    cb = mmgroup(psM0[:, 0:512],
                                 [(ONE[0:1, :],
                                   CROW[0:1, tg * 512:(tg + 1) * 512])],
                                 "psM0", deps=[crow_tok] + prev_ev)
                    prev_cb = cb
                    cbe = p.em("scalar", lambda e: e.copy(
                        CBC[:, :], psM0[:, 0:512]), deps=[cb])
                    evac("psM0", cbe)
                    htoks = []
                    for f in range(16):
                        slot = "psA0" if f % 2 == 0 else "psA1"
                        tok = mmgroup(
                            (psA0 if f % 2 == 0 else psA1)[:, :],
                            [(T2[:, c * 2048 + f * 128:
                                 c * 2048 + f * 128 + 128],
                              LNE[:, c * 512:(c + 1) * 512])
                             for c in range(NC_)], slot,
                            deps=w1toks + le_toks)
                        ge = p.em("scalar", lambda e, f=f, e_i=e_i, sub=sub:
                                  e.activation(
                                      out=T4[:, f * 512:(f + 1) * 512],
                                      in_=(psA0 if f % 2 == 0
                                           else psA1)[:, :],
                                      func=AF.Gelu_apprx_tanh,
                                      bias=bvc("fc1b",
                                               e_i * 32 + sub * 16 + f)),
                                  deps=[tok])
                        evac(slot, ge)
                        htoks.append(ge)
                    prev_fc1_last = tok
                    for half in range(2):
                        ytoks = []
                        for ec in range(4):
                            pairs = [(T3[:, f * 1024 + (half * 4 + ec) * 128:
                                         f * 1024 + (half * 4 + ec) * 128
                                         + 128],
                                      T4[:, f * 512:(f + 1) * 512])
                                     for f in range(16)]
                            ytoks.append(mmgroup(
                                psY[:, ec * 512:(ec + 1) * 512], pairs,
                                "psY", deps=w2toks + htoks))
                        ev = []
                        for ec in range(4):
                            ecg = half * 4 + ec
                            if sub == 1:
                                t1_ = p.em("vector",
                                           lambda e, ec=ec, ecg=ecg, e_i=e_i:
                                           e.scalar_tensor_tensor(
                                               STGa[:, 0:512],
                                               psY[:, ec * 512:
                                                   (ec + 1) * 512],
                                               bvc("fc2b", e_i * 8 + ecg),
                                               CBC[:, :], OP.add, OP.mult),
                                           deps=ytoks + [cbe])
                            else:
                                t1_ = p.em("vector", lambda e, ec=ec:
                                           e.tensor_mul(
                                               STGa[:, 0:512],
                                               psY[:, ec * 512:
                                                   (ec + 1) * 512],
                                               CBC[:, :]),
                                           deps=ytoks + [cbe])
                            t2_ = p.em("vector", lambda e, ecg=ecg, tg=tg:
                                       e.tensor_add(
                                           XT[:, ecg * NT + tg * 512:
                                              ecg * NT + tg * 512 + 512],
                                           XT[:, ecg * NT + tg * 512:
                                              ecg * NT + tg * 512 + 512],
                                           STGa[:, 0:512]), deps=[t1_])
                            ev.append(t2_)
                        evac("psY", ev[-1])
                        prev_ev = ev
                p.barrier()

        fmln2(lambda c, lo, n: XT[:, c * NT + lo: c * NT + lo + n],
              lambda c, lo, n: LNT[:, c * NT + lo: c * NT + lo + n], NT)
        p.barrier()
        fmln2(lambda c, lo, n: LNT[:, c * NT + lo: c * NT + lo + n],
              lambda c, lo, n: T2[:, c * NT + lo: c * NT + lo + n], NT,
              g=lambda c: bvc("g1", c), b=lambda c: bvc("be1", c))
        p.barrier()
        kv_dma = []
        for hg in range(NC_):
            wtoks = []
            for c in range(NC_):
                wtoks.append(loadcast(
                    inp["qkvw"][ts(c), 1024 + hg * 128: 1024 + hg * 128 + 128],
                    T4[:, c * 128: c * 128 + 128], 128))
                wtoks.append(loadcast(
                    inp["qkvw"][ts(c), 2048 + hg * 128: 2048 + hg * 128 + 128],
                    T4[:, 1024 + c * 128: 1024 + c * 128 + 128], 128))
            kvtoks = []
            for tcol in range(4):
                ktok = mmgroup(psA0[:, :],
                               [(T4[:, c * 128: c * 128 + 128],
                                 T2[:, c * NT + tcol * 512:
                                    c * NT + tcol * 512 + 512])
                                for c in range(NC_)], "psA0", deps=wtoks)
                e1 = p.em("vector", lambda e, hg=hg: e.tensor_scalar(
                    PTB[:, 0:512], psA0[:, :], bvc("qkvbk", hg), 0.0,
                    OP.add, OP.min), deps=[ktok])
                e2 = p.em("scalar", lambda e: e.activation(
                    out=PSB2[:, 0:512], in_=PTB[:, 0:512], func=AF.Exp),
                    deps=[e1])
                e3 = p.em("scalar", lambda e, hg=hg: e.activation(
                    out=PTB[:, 0:512], in_=psA0[:, :], func=AF.Relu,
                    bias=bvc("qkvbk", hg)), deps=[e2])
                evac("psA0", e3)
                e4 = p.em("vector", lambda e, tcol=tcol: e.tensor_add(
                    T3[:, tcol * 512: tcol * 512 + 512], PTB[:, 0:512],
                    PSB2[:, 0:512]), deps=[e2, e3])
                vtok = mmgroup(psA1[:, :],
                               [(T4[:, 1024 + c * 128: 1024 + c * 128 + 128],
                                 T2[:, c * NT + tcol * 512:
                                    c * NT + tcol * 512 + 512])
                                for c in range(NC_)], "psA1",
                               deps=wtoks + [e1])
                e5 = p.em("scalar", lambda e, tcol=tcol, hg=hg: e.activation(
                    out=T3[:, 2048 + tcol * 512: 2048 + tcol * 512 + 512],
                    in_=psA1[:, :], func=AF.Identity, bias=bvc("qkvbv", hg)),
                    deps=[vtok])
                evac("psA1", e5)
                kvtoks += [e4, e5]
            tr = []
            for t in range(16):
                tr.append(transp(T3[:, t * 128: t * 128 + 128],
                                 T3[:, 4096 + t * 128: 4096 + t * 128 + 128],
                                 deps=kvtoks))
                tr.append(transp(T3[:, 2048 + t * 128: 2048 + t * 128 + 128],
                                 T3[:, 6144 + t * 128: 6144 + t * 128 + 128],
                                 deps=kvtoks))
            vp = []
            for t in range(16):
                base = 8192 + t * 132
                vp.append(p.em("vector", lambda e, b=base, t=t:
                               e.tensor_copy(
                    T3[:, b: b + 64],
                    T3[:, 6144 + t * 128: 6144 + t * 128 + 64]), deps=tr))
                vp.append(p.em("vector", lambda e, b=base: e.memset(
                    T3[:, b + 64: b + 66], 1.0), deps=tr))
                vp.append(p.em("vector", lambda e, b=base, t=t: e.tensor_copy(
                    T3[:, b + 66: b + 130],
                    T3[:, 6144 + t * 128 + 64: 6144 + t * 128 + 128]),
                    deps=tr))
                vp.append(p.em("vector", lambda e, b=base: e.memset(
                    T3[:, b + 130: b + 132], 1.0), deps=tr))
            kvm = mmgroup(psM0[:, 0:132],
                          [(T3[:, 4096 + t * 128: 4096 + t * 128 + 128],
                            T3[:, 8192 + t * 132: 8192 + t * 132 + 132])
                           for t in range(16)], "psM0", deps=vp)
            kve = p.em("vector", lambda e: e.tensor_copy(
                KVS[:, 0:65], psM0[:, 0:65]), deps=[kvm])
            kve = p.em("vector", lambda e: e.tensor_copy(
                KVS[:, 65:130], psM0[:, 66:131]), deps=[kvm, kve])
            evac("psM0", kve)
            kv_dma.append(p.dma("gpsimd", outs["kvz"][:, hg * 130:
                                                      hg * 130 + 130],
                                KVS[:, :], deps=[kve]))
            p.barrier()
        for c in range(NC_):
            p.dma("gpsimd", outs["xmid"][ts(c), :], XT[:, ts(c, NT)])
        p.barrier()
    else:
        for c in range(NC_):
            p.dma("gpsimd", XT[:, ts(c, NT)], inp["xmid"][ts(c), :],
                  deps=base)
        k0 = p.dma("gpsimd", KV2a[:, :], inp["kvz0"], deps=base)
        k1 = p.dma("gpsimd", KV2b[:, :], inp["kvz1"], deps=base)
        ks = p.em("vector", lambda e: e.tensor_add(
            KV2a[:, :], KV2a[:, :], KV2b[:, :]), deps=[k0, k1])
        kc = p.em("scalar", lambda e: e.copy(KVB[:, :], KV2a[:, :]),
                  deps=[ks])
        p.barrier()
        fmln2(lambda c, lo, n: XT[:, c * NT + lo: c * NT + lo + n],
              lambda c, lo, n: LNT[:, c * NT + lo: c * NT + lo + n], NT)
        p.barrier()
        fmln2(lambda c, lo, n: LNT[:, c * NT + lo: c * NT + lo + n],
              lambda c, lo, n: T2[:, c * NT + lo: c * NT + lo + n], NT,
              g=lambda c: bvc("g1", c), b=lambda c: bvc("be1", c))
        p.barrier()
        qf_toks = []
        for hg in range(NC_):
            p.barrier()
            wtoks = [loadcast(
                inp["qkvw"][ts(c), hg * 128: hg * 128 + 128],
                T4[:, c * 128: c * 128 + 128], 128) for c in range(NC_)]
            for tcol in range(4):
                qtok = mmgroup(psA0[:, :],
                               [(T4[:, c * 128: c * 128 + 128],
                                 T2[:, c * NT + tcol * 512:
                                    c * NT + tcol * 512 + 512])
                                for c in range(NC_)], "psA0",
                               deps=wtoks + qf_toks[-1:])
                e1 = p.em("vector", lambda e, hg=hg: e.tensor_scalar(
                    PTB[:, 0:512], psA0[:, :], bvc("qkvbq", hg), 0.0,
                    OP.add, OP.min), deps=[qtok])
                e2 = p.em("scalar", lambda e: e.activation(
                    out=PSB2[:, 0:512], in_=PTB[:, 0:512], func=AF.Exp),
                    deps=[e1])
                e3 = p.em("scalar", lambda e, hg=hg: e.activation(
                    out=PTB[:, 0:512], in_=psA0[:, :], func=AF.Relu,
                    bias=bvc("qkvbq", hg)), deps=[e2])
                evac("psA0", e3)
                qf_toks.append(p.em("vector",
                               lambda e, hg=hg, tcol=tcol: e.tensor_add(
                                   T3[:, hg * 2048 + tcol * 512:
                                      hg * 2048 + tcol * 512 + 512],
                                   PTB[:, 0:512], PSB2[:, 0:512]),
                               deps=[e2, e3]))
        p.barrier()
        prev_t3 = None
        yt = []
        for hg in range(NC_):
            for h in range(2):
                for tt in range(16):
                    tok = mmgroup(
                        psM0[:, 0:65],
                        [(T3[h * 64:h * 64 + 64,
                             hg * 2048 + tt * 128: hg * 2048 + tt * 128 + 128],
                          KVB[h * 64:h * 64 + 64,
                              hg * 130 + h * 65: hg * 130 + h * 65 + 65])],
                        "psM0", deps=[kc] + qf_toks)
                    t1 = p.em("vector", lambda e: e.tensor_scalar(
                        GSC[:, 0:1], psM0[:, 64:65], 1.0e-6, None, OP.max),
                        deps=[tok, prev_t3])
                    t2 = p.em("vector", lambda e: e.reciprocal(
                        GSC[:, 1:2], GSC[:, 0:1]), deps=[t1])
                    t3 = p.em("scalar", lambda e, hg=hg, h=h, tt=tt:
                              e.activation(
                                  out=T2[:, tt * 1024 + hg * 128 + h * 64:
                                         tt * 1024 + hg * 128 + h * 64 + 64],
                                  in_=psM0[:, 0:64], func=AF.Copy,
                                  scale=GSC[:, 1:2]), deps=[t2])
                    evac("psM0", t3)
                    prev_t3 = t3
                    yt.append(t3)
        p.barrier()
        ytr = []
        for tt in range(16):
            for c in range(NC_):
                ytr.append(transp(
                    T2[:, tt * 1024 + c * 128: tt * 1024 + c * 128 + 128],
                    T3[:, c * 2048 + tt * 128: c * 2048 + tt * 128 + 128]))
        for ec in range(NC_):
            p.barrier()
            wtoks = [loadcast(inp["outw"][ts(c), ts(ec)],
                              T4[:, c * 128: c * 128 + 128], 128)
                     for c in range(NC_)]
            for tcol in range(4):
                tok = mmgroup(psA0[:, :],
                              [(T4[:, c * 128: c * 128 + 128],
                                T3[:, c * 2048 + tcol * 512:
                                   c * 2048 + tcol * 512 + 512])
                               for c in range(NC_)], "psA0",
                              deps=wtoks + ytr)
                et = p.em("vector", lambda e, ec=ec, tcol=tcol:
                          e.scalar_tensor_tensor(
                              LNT[:, ec * NT + tcol * 512:
                                 ec * NT + tcol * 512 + 512],
                              psA0[:, :], bvc("outb", ec),
                              LNT[:, ec * NT + tcol * 512:
                                 ec * NT + tcol * 512 + 512],
                              OP.add, OP.add), deps=[tok])
                evac("psA0", et)
        p.barrier()
        for c in range(NC_):
            p.em("vector", lambda e, c=c: e.tensor_add(
                XT[:, ts(c, NT)], XT[:, ts(c, NT)], LNT[:, ts(c, NT)]))
        p.barrier()
        fmln2(lambda c, lo, n: LNT[:, c * NT + lo: c * NT + lo + n],
              lambda c, lo, n: T2[:, c * NT + lo: c * NT + lo + n], NT,
              g=lambda c: bvc("g2", c), b=lambda c: bvc("be2", c))
        p.barrier()
        for fq in range(4):
            w1toks = [loadcast(inp["w1"][ts(c), fq * 1024:(fq + 1) * 1024],
                               T3[:, c * 1024:(c + 1) * 1024], 1024)
                      for c in range(NC_)]
            w2toks = [loadcast(
                inp["w2"][fq * 1024 + f * 128: fq * 1024 + (f + 1) * 128, :],
                T4[:, f * 1024:(f + 1) * 1024], 1024) for f in range(8)]
            for tg in range(4):
                htoks = []
                for f in range(8):
                    slot = "psA0" if f % 2 == 0 else "psA1"
                    tok = mmgroup(
                        (psA0 if f % 2 == 0 else psA1)[:, :],
                        [(T3[:, c * 1024 + f * 128: c * 1024 + f * 128 + 128],
                          T2[:, c * NT + tg * 512: c * NT + tg * 512 + 512])
                         for c in range(NC_)], slot, deps=w1toks)
                    ge = p.em("scalar", lambda e, f=f, fq=fq: e.activation(
                        out=T3[:, 8192 + f * 512: 8192 + f * 512 + 512],
                        in_=(psA0 if f % 2 == 0 else psA1)[:, :],
                        func=AF.Gelu, bias=bvc("b1", fq * 8 + f)),
                        deps=[tok])
                    evac(slot, ge)
                    htoks.append(ge)
                for half in range(2):
                    ytoks = []
                    for ec in range(4):
                        pairs = [(T4[:, f * 1024 + (half * 4 + ec) * 128:
                                     f * 1024 + (half * 4 + ec) * 128 + 128],
                                  T3[:, 8192 + f * 512: 8192 + f * 512 + 512])
                                 for f in range(8)]
                        ytoks.append(mmgroup(
                            psY[:, ec * 512:(ec + 1) * 512], pairs, "psY",
                            deps=w2toks + htoks))
                    ev = []
                    for ec in range(4):
                        ecg = half * 4 + ec
                        if fq == 3:
                            t2_ = p.em("vector",
                                       lambda e, ec=ec, ecg=ecg, tg=tg:
                                       e.scalar_tensor_tensor(
                                           XT[:, ecg * NT + tg * 512:
                                              ecg * NT + tg * 512 + 512],
                                           psY[:, ec * 512:(ec + 1) * 512],
                                           bvc("b2", ecg),
                                           XT[:, ecg * NT + tg * 512:
                                              ecg * NT + tg * 512 + 512],
                                           OP.add, OP.add), deps=ytoks)
                        else:
                            t2_ = p.em("vector",
                                       lambda e, ec=ec, ecg=ecg, tg=tg:
                                       e.tensor_add(
                                           XT[:, ecg * NT + tg * 512:
                                              ecg * NT + tg * 512 + 512],
                                           XT[:, ecg * NT + tg * 512:
                                              ecg * NT + tg * 512 + 512],
                                           psY[:, ec * 512:(ec + 1) * 512]),
                                       deps=ytoks)
                        ev.append(t2_)
                    evac("psY", ev[-1])
            p.barrier()

        fmln2(lambda c, lo, n: XT[:, c * NT + lo: c * NT + lo + n],
              lambda c, lo, n: LNT[:, c * NT + lo: c * NT + lo + n], NT)
        p.barrier()

        def lnc(c, lo, n):
            return LNT[:, c * NT + lo: c * NT + lo + n]
        wg_toks = [loadcast(inp["WgB"][ts(c), :], T4[:, c * 8: c * 8 + 8], 8)
                   for c in range(NC_)]
        gtoks = []
        for tt in range(16):
            tok = mmgroup(psM0[:, 0:8],
                          [(lnc(c, tt * 128, 128), T4[:, c * 8: c * 8 + 8])
                           for c in range(NC_)], "psM0", deps=wg_toks)
            gtoks.append(p.em("scalar", lambda e, tt=tt: e.copy(
                LOG[:, tt * 8: tt * 8 + 8], psM0[:, 0:8]), deps=[tok]))
            evac("psM0", gtoks[-1])
        cmb_toks = []
        prev_tt2 = None
        for tt in range(16):
            lg = LOG[:, tt * 8: tt * 8 + 8]
            t1 = p.em("vector", lambda e, lg=lg: e.reduce_max(
                GSC[:, 8:9], lg, AX, negate=True),
                deps=gtoks + ([prev_tt2] if prev_tt2 else []))
            tE = p.em("scalar", lambda e, lg=lg: e.activation(
                out=GSC[:, 16:24], in_=lg, func=AF.Exp, bias=GSC[:, 8:9]),
                deps=[t1])
            tS = p.em("vector", lambda e: e.reduce_sum(
                GSC[:, 24:25], GSC[:, 16:24], AX), deps=[tE])
            t2 = p.em("vector", lambda e, lg=lg: e.tensor_scalar(
                GSC[:, 32:40], lg, GSC[:, 8:9], 0.0, OP.add, OP.is_equal),
                deps=[t1])
            t3 = p.em("vector", lambda e, lg=lg: e.scalar_tensor_tensor(
                GSC[:, 40:48], GSC[:, 32:40], NEG, lg, OP.mult, OP.add),
                deps=[t2])
            t4 = p.em("vector", lambda e: e.reduce_max(
                GSC[:, 9:10], GSC[:, 40:48], AX, negate=True), deps=[t3])
            t5 = p.em("vector", lambda e: e.tensor_sub(
                GSC[:, 10:11], GSC[:, 8:9], GSC[:, 9:10]), deps=[t4])
            t6 = p.em("scalar", lambda e: e.activation(
                out=GSC[:, 11:12], in_=GSC[:, 10:11], func=AF.Exp),
                deps=[t5])
            t7 = p.em("vector", lambda e: e.scalar_tensor_tensor(
                GSC[:, 12:13], GSC[:, 24:25], 1.0e-8, GSC[:, 11:12],
                OP.mult, OP.add), deps=[t6, tS])
            t7b = p.em("vector", lambda e: e.tensor_scalar(
                GSC[:, 12:13], GSC[:, 12:13], 1.0, None, OP.add), deps=[t7])
            t8 = p.em("vector", lambda e: e.reciprocal(
                GSC[:, 13:14], GSC[:, 12:13]), deps=[t7b])
            t9 = p.em("vector", lambda e: e.tensor_mul(
                GSC[:, 14:15], GSC[:, 11:12], GSC[:, 13:14]), deps=[t8])
            ta = p.em("vector", lambda e, lg=lg: e.tensor_scalar(
                GSC[:, 48:56], lg, GSC[:, 9:10], 0.0, OP.add, OP.is_equal),
                deps=[t4])
            tb = p.em("vector", lambda e: e.tensor_scalar(
                GSC[:, 56:64], GSC[:, 48:56], GSC[:, 14:15], None, OP.mult),
                deps=[t9, ta])
            tc = p.em("vector", lambda e: e.scalar_tensor_tensor(
                GSC[:, 0:8], GSC[:, 32:40], GSC[:, 13:14], GSC[:, 56:64],
                OP.mult, OP.add), deps=[tb])
            tt1 = p.em("tensor", lambda e: e.transpose(
                psM0[0:8, 0:128], GSC[:, 0:8], ID3[:]),
                deps=[tc, last["psM0"]])
            tt2 = p.em("scalar", lambda e, tt=tt: e.copy(
                CMT[0:8, tt * 128: tt * 128 + 128], psM0[0:8, 0:128]),
                deps=[tt1])
            evac("psM0", tt2)
            prev_tt2 = tt2
            cmb_toks.append(tt2)
        p.barrier()
        prev_ev = []
        prev_cb = None
        for e_i in range(8):
            crow_tok = p.dma("gpsimd", CROW[0:1, :], CMT[e_i:e_i + 1, :],
                             deps=cmb_toks + [prev_cb])
            w1toks = [loadcast(inp["W1e"][e_i, ts(c), :],
                               T2[:, c * 2048:(c + 1) * 2048], 2048)
                      for c in range(NC_)]
            w2toks = [loadcast(inp["W2e"][e_i, ts(f), :],
                               T3[:, f * 1024:(f + 1) * 1024], 1024)
                      for f in range(16)]
            for tg in range(4):
                cb = mmgroup(psM0[:, 0:512],
                             [(ONE[0:1, :],
                               CROW[0:1, tg * 512:(tg + 1) * 512])],
                             "psM0", deps=[crow_tok] + prev_ev)
                prev_cb = cb
                cbe = p.em("scalar", lambda e: e.copy(
                    CBC[:, :], psM0[:, 0:512]), deps=[cb])
                evac("psM0", cbe)
                htoks = []
                for f in range(16):
                    slot = "psA0" if f % 2 == 0 else "psA1"
                    tok = mmgroup(
                        (psA0 if f % 2 == 0 else psA1)[:, :],
                        [(T2[:, c * 2048 + f * 128: c * 2048 + f * 128 + 128],
                          lnc(c, tg * 512, 512)) for c in range(NC_)],
                        slot, deps=w1toks)
                    ge = p.em("scalar", lambda e, f=f, e_i=e_i: e.activation(
                        out=T4[:, f * 512:(f + 1) * 512],
                        in_=(psA0 if f % 2 == 0 else psA1)[:, :],
                        func=AF.Relu, bias=bvc("b1e", e_i * 16 + f)),
                        deps=[tok])
                    evac(slot, ge)
                    htoks.append(ge)
                for half in range(2):
                    ytoks = []
                    for ec in range(4):
                        pairs = [(T3[:, f * 1024 + (half * 4 + ec) * 128:
                                     f * 1024 + (half * 4 + ec) * 128 + 128],
                                  T4[:, f * 512:(f + 1) * 512])
                                 for f in range(16)]
                        ytoks.append(mmgroup(
                            psY[:, ec * 512:(ec + 1) * 512], pairs, "psY",
                            deps=w2toks + htoks))
                    ev = []
                    for ec in range(4):
                        ecg = half * 4 + ec
                        t1_ = p.em("vector", lambda e, ec=ec, ecg=ecg,
                                   e_i=e_i: e.scalar_tensor_tensor(
                                       STGa[:, 0:512],
                                       psY[:, ec * 512:(ec + 1) * 512],
                                       bvc("b2e", e_i * 8 + ecg), CBC[:, :],
                                       OP.add, OP.mult),
                                   deps=ytoks + [cbe])
                        t2_ = p.em("vector", lambda e, ecg=ecg, tg=tg:
                                   e.tensor_add(
                                       XT[:, ecg * NT + tg * 512:
                                          ecg * NT + tg * 512 + 512],
                                       XT[:, ecg * NT + tg * 512:
                                          ecg * NT + tg * 512 + 512],
                                       STGa[:, 0:512]), deps=[t1_])
                        ev.append(t2_)
                    evac("psY", ev[-1])
                    prev_ev = ev
            p.barrier()
        fin = [p.em("vector", lambda e, c=c: e.tensor_add(
            XT[:, ts(c, NT)], XT[:, ts(c, NT)], LNT[:, ts(c, NT)]))
            for c in range(NC_)]
        prev_dma = None
        for c in range(NC_):
            for piece in range(2):
                lo = piece * 1024
                s = STGa if piece == 0 else STGb
                ct = p.em("scalar", lambda e, c=c, lo=lo, s=s: e.activation(
                    out=s[:, 0:1024],
                    in_=XT[:, c * NT + lo: c * NT + lo + 1024],
                    func=AF.Copy), deps=fin + [prev_dma])
                prev_dma = p.dma("gpsimd", outs["out"][ts(c), lo:lo + 1024],
                                 s[:, 0:1024], deps=[ct])
        p.barrier()

    with nc.Block() as block:
        p.sems = {}
        for e in ENGS:
            h = nc.semaphore(f"sem_{e}")
            p.sems[e] = h.__enter__()
            ctx.append(h)
        p.finalize(block)
    for h in reversed(ctx):
        h.__exit__(None, None, None)
    return nc


_progs = {}


def get_prog(part):
    if part not in _progs:
        _progs[part] = build(part)
    return _progs[part]


def _mask_blocks(h):
    out = np.zeros((128, 5 * 768), dtype=np.float32)
    qts = {0: 4, 1: 0, 2: 1, 3: 14, 4: 15}
    for blk, qt in qts.items():
        i = np.arange(128)[:, None]
        j = np.arange(768)[None, :]
        gq = h * 2048 + qt * 128 + i
        gk = h * 2048 - 256 + qt * 128 + j
        valid = (np.abs(gq - gk) <= 256) & (gk >= 0) & (gk < 4096)
        out[:, blk * 768:(blk + 1) * 768] = np.where(valid, 0.0, NEG)
    return out.astype(ml_dtypes.bfloat16)


def kernel(**kw):
    bf = ml_dtypes.bfloat16
    x = kw["L0_attn_hidden_states"]
    idbf = np.eye(128, dtype=bf)
    id32 = np.eye(128, dtype=np.float32)
    onesbf = np.ones((128, 128), dtype=bf)

    def pp(v):
        return np.ascontiguousarray(v.reshape(8, 128).T).astype(np.float32)

    bvec = np.zeros((128, BVW), dtype=np.float32)

    def put(name, arr):
        bvec[:, BV[name]:BV[name] + arr.shape[1]] = arr
    put("bq", pp(kw["L0_attn_bq"]))
    put("bk", pp(kw["L0_attn_bk"]))
    put("bv", pp(kw["L0_attn_bv"]))
    put("bo", pp(kw["L0_attn_bo"]))
    put("lnw", np.concatenate([pp(kw["L0_ffn_ln_w"][e]) for e in range(8)],
                              axis=1))
    put("lnb", np.concatenate([pp(kw["L0_ffn_ln_b"][e]) for e in range(8)],
                              axis=1))
    put("fc1b", np.concatenate(
        [np.ascontiguousarray(kw["L0_ffn_fc1_b"][e].reshape(32, 128).T)
         for e in range(8)], axis=1).astype(np.float32))
    put("fc2b", np.concatenate([pp(kw["L0_ffn_fc2_b"][e]) for e in range(8)],
                               axis=1))
    a = np.zeros((128, 1), np.float32)
    a[0:8, 0] = kw["L0_ffn_alpha"]
    put("alpha", a)
    qb = kw["L1_attn_qkv_b"]
    put("qkvbq", pp(qb[0:1024]))
    put("qkvbk", pp(qb[1024:2048]))
    put("qkvbv", pp(qb[2048:3072]))
    put("outb", pp(kw["L1_attn_out_b"]))
    put("b1", np.ascontiguousarray(
        kw["L1_attn_b1"].reshape(32, 128).T).astype(np.float32))
    put("b2", pp(kw["L1_attn_b2"]))
    put("g1", pp(kw["L1_attn_g1"]))
    put("be1", pp(kw["L1_attn_be1"]))
    put("g2", pp(kw["L1_attn_g2"]))
    put("be2", pp(kw["L1_attn_be2"]))
    put("b1e", np.concatenate(
        [np.ascontiguousarray(kw["L1_ffn_b1"][e].reshape(16, 128).T)
         for e in range(8)], axis=1).astype(np.float32))
    put("b2e", np.concatenate([pp(kw["L1_ffn_b2"][e]) for e in range(8)],
                              axis=1))

    ncA = get_prog("A")
    in_maps = []
    for core in range(8):
        b, h = core // 2, core % 2
        lo = h * 2048
        seg = np.zeros((2560, 1024), dtype=np.float32)
        seg[256:2304] = x[b, lo:lo + 2048]
        if h == 1:
            seg[0:256] = x[b, 1792:2048]
        if h == 0:
            seg[2304:2560] = x[b, 2048:2304]
        in_maps.append({
            "xt": np.ascontiguousarray(seg.T),
            "wq": kw["L0_attn_wq"], "wk": kw["L0_attn_wk"],
            "wv": kw["L0_attn_wv"], "wo": kw["L0_attn_wo"],
            "Wg": kw["L0_ffn_Wg"], "fc1w": kw["L0_ffn_fc1_w"],
            "fc2w": kw["L0_ffn_fc2_w"], "qkvw": kw["L1_attn_qkv_w"],
            "mask": _mask_blocks(h), "bvec": bvec,
            "idbf": idbf, "id32": id32, "onesbf": onesbf,
        })
    resA = run_bass_kernel_spmd(ncA, in_maps, core_ids=list(range(8)))
    xmids = [resA.results[c]["xmid"] for c in range(8)]
    kvzs = [resA.results[c]["kvz"] for c in range(8)]

    ncB = get_prog("B")
    in_maps = []
    for core in range(8):
        in_maps.append({
            "xmid": xmids[core], "kvz0": kvzs[core], "kvz1": kvzs[core ^ 1],
            "qkvw": kw["L1_attn_qkv_w"], "outw": kw["L1_attn_out_w"],
            "w1": kw["L1_attn_w1"], "w2": kw["L1_attn_w2"],
            "WgB": kw["L1_ffn_gate_W"], "W1e": kw["L1_ffn_W1"],
            "W2e": kw["L1_ffn_W2"], "bvec": bvec,
            "idbf": idbf, "id32": id32, "onesbf": onesbf,
        })
    resB = run_bass_kernel_spmd(ncB, in_maps, core_ids=list(range(8)))
    out = np.zeros((4, 4096, 1024), dtype=np.float32)
    for core in range(8):
        b, h = core // 2, core % 2
        out[b, h * 2048:(h + 1) * 2048] = resB.results[core]["out"].T
    return out
